# version 03 (final)
# baseline (speedup 1.0000x reference)
import contextlib

import numpy as np
import ml_dtypes
import concourse.bass as bass
import concourse.mybir as mybir
from concourse.bass import ds, ts
from concourse.bass_utils import run_bass_kernel_spmd

F32 = mybir.dt.float32
BF16 = mybir.dt.bfloat16
I32 = mybir.dt.int32
U8 = mybir.dt.uint8
AF = mybir.ActivationFunctionType
ALU = mybir.AluOpType
AX = mybir.AxisListType

S = 4096
D = 4096
H = 32
HD = 128
NHC = D // 512
NK = D // 128
A2 = 2560
ISQ = float(1.0 / np.sqrt(128.0))
EPS = 1e-5

PAIR = [
    dict(
        N=1280, KVR=1280, KVL=2560,
        QG=[(0, 1024), (1024, 256)],
        KVG=[(0, 1024), (1024, 256)],
        QCS=[(0, 512, 8), (512, 512, 16), (1024, 256, 20)],
    ),
    dict(
        N=768, KVR=2048, KVL=4096,
        QG=[(0, 768)],
        KVG=[(0, 1024), (1024, 1024)],
        QCS=[(0, 512, 28), (512, 256, 32)],
    ),
]
NMAX = 1280
KVRMAX = 2048
ARENA = 190 * 1024

_DEV_PARTS = {"memset": True, "phase_a": True, "barrier": True,
              "bar_wflag": True, "bar_poll": True,
              "phase_c": True, "phase_d": True, "phase_e": True}
_DEV_LIMIT = {"passes": "kvq", "maxgrp": 8, "cps": 8}

SEM_NAMES = [
    "xt", "xtf", "wslf", "pb", "pba", "pbd", "rotc", "t2d",
    "wsl0", "wsl1", "wsl2", "wsl3", "wsl4", "wsl5", "wsl6", "wsl7",
    "obw0", "obw1", "obw2", "obw3",
    "dvec", "ob", "obv", "cst", "kvw", "flg",
    "kld0", "kld1", "vld0", "vld1", "qld0", "qld1", "sc", "expd", "mskd", "rzd", "pvd", "ctxd", "mld",
    "wo0", "wo1", "wod", "doe", "dw0", "dw1", "lnd", "lna", "x1w", "x1bw",
    "x1t", "up0", "up1", "upd", "zgd", "ohd", "eoe", "ew0", "ew1",
    "ffw", "yw", "actc", "plc", "bar",
]


class Ectx:

    def __init__(self, nc, e, engname, sems, parity=None):
        self.nc = nc
        self.e = e
        self.name = engname
        self.sems = sems
        self.parity = parity
        self.cnt = {k: 0 for k in sems}

    def mine(self, owner):
        return owner == self.name

    def kvoff(self, e, KVR):
        if not hasattr(self, "_kvoff"):
            pid = self.nc.partition_id(engines=[e.engine])
            self._kvoff = (pid % 2) * KVR
        return self._kvoff

    def emit(self, owner, fn, incs=(), waits=()):
        assert len(incs) <= 1, incs
        if self.mine(owner):
            for s, val in waits:
                self.e.wait_ge(self.sems[s], val)
            inst = fn(self.e)
            for s, n in incs:
                inst = inst.then_inc(self.sems[s], n)
        for s, n in incs:
            self.cnt[s] += n



def declare_io(nc, debug=False):
    io = {}

    def inp(name, shape, dt=F32):
        io[name] = nc.dram_tensor(name, list(shape), dt, kind="ExternalInput").ap()

    inp("x_own", (NMAX, D))
    inp("xT_q", (D, NMAX))
    inp("xT_kv", (D, KVRMAX))
    inp("cos_q", (NMAX, 512))
    inp("sin_q", (NMAX, 512))
    inp("cos_kv", (KVRMAX, 512))
    inp("sin_kv", (KVRMAX, 512))
    inp("wq", (D, D))
    inp("wk", (D, D))
    inp("wv", (D, D))
    inp("wo", (D, D))
    inp("wup", (D, D))
    inp("ag", (D, 36))
    inp("bias128", (128, 4))
    inp("lorabf", (32, D))
    inp("id128", (128, 128))
    inp("dmaskA", (8, 128, 512))
    inp("dmaskB", (4, 128, 256))
    inp("cfg", (1, 16), I32)
    inp("token", (1, 16), I32)

    io["y"] = nc.dram_tensor("y", [NMAX, D], F32, kind="ExternalOutput").ap()

    kind_dbg = "ExternalOutput" if debug else "Internal"
    shr = "Local" if debug else "Shared"
    io["k_tok"] = nc.dram_tensor("k_tok", [S, D], BF16, kind=kind_dbg, addr_space=shr).ap()
    io["v_tok"] = nc.dram_tensor("v_tok", [S, D], BF16, kind=kind_dbg, addr_space=shr).ap()
    io["flags"] = nc.dram_tensor("flags", [2, 16], I32, kind="Internal", addr_space=shr).ap()
    io["q_tok"] = nc.dram_tensor("q_tok", [NMAX, D], BF16, kind=kind_dbg).ap()
    io["attn_s"] = nc.dram_tensor("attn_s", [NMAX, D], F32, kind="Internal").ap()
    io["x1_s"] = nc.dram_tensor("x1_s", [NMAX, D], F32, kind="Internal").ap()
    io["x1b_s"] = nc.dram_tensor("x1b_s", [NMAX, D], BF16, kind="Internal").ap()
    io["ffn_s"] = nc.dram_tensor("ffn_s", [NMAX, D], F32, kind="Internal").ap()
    return io



class SbufPlan:
    def __init__(self, arena, v):
        self.arena = arena
        self.v = v
        self.off = 0
        self.t = {}

    def alloc(self, name, shape, dt, align=64):
        esz = mybir.dt.size(dt)
        cols = int(np.prod(shape[1:]))
        nbytes = cols * esz
        self.off = (self.off + align - 1) // align * align
        assert self.off % esz == 0
        view = self.arena.bitcast(dt)
        ap = view.ap()[0:shape[0], self.off // esz: self.off // esz + cols]
        if len(shape) > 2:
            pat = "p (" + " ".join(f"d{i}" for i in range(1, len(shape))) + ") -> p " + \
                  " ".join(f"d{i}" for i in range(1, len(shape)))
            kw = {f"d{i}": shape[i] for i in range(1, len(shape))}
            ap = ap.rearrange(pat, **kw)
        self.t[name] = ap
        self.off += nbytes
        assert self.off <= ARENA, (name, self.off)
        return ap

    def mark(self):
        return self.off

    def reset(self, off):
        self.off = off


def plan_sbuf(arena, v):
    P = PAIR[v]
    sp = SbufPlan(arena, v)
    sp.alloc("lorab_sb", (32, D), BF16)
    sp.alloc("ag_sb", (128, NK * 36), BF16)
    sp.alloc("bias_sb", (128, 4), F32)
    sp.alloc("tok_sb", (1, 16), I32)
    sp.alloc("ones128", (128, 128), BF16)
    sp.alloc("id_sb", (128, 128), BF16)
    base0 = sp.mark()

    sp.alloc("xt_g", (128, NK * 1024), BF16)
    sp.alloc("wsl", (128, 8 * 512), BF16)
    sp.alloc("cosT", (128, 8 * 512), F32)
    sp.alloc("sinT", (128, 8 * 512), F32)
    sp.alloc("ptmp", (128, 2 * 512), F32)
    sp.alloc("rot", (128, 2 * 512), F32)
    sp.alloc("t1", (128, 2 * 512), F32)
    sp.alloc("t2", (128, 2 * 512), F32)
    sp.alloc("obuf", (128, 4 * 512), BF16)
    a_end = sp.mark()

    sp.reset(base0)
    sp.alloc("ctx", (128, H * P["N"]), BF16)
    baseC = sp.mark()
    sp.alloc("kt", (128, 2 * P["KVL"]), BF16)
    sp.alloc("vt", (128, 2 * P["KVL"]), BF16)
    sp.alloc("qt", (128, 2 * P["N"]), BF16)
    nkbmax = P["QCS"][-1][2]
    sp.alloc("expt", (128, 2 * nkbmax * 512), BF16)
    sp.alloc("mskA", (128, 8 * 512), BF16)
    sp.alloc("mskB", (128, 4 * 256), BF16)
    sp.alloc("rzb", (128, 2 * 512), F32)
    sp.alloc("zln", (128, 2 * 512), F32)
    c_end = sp.mark()

    sp.reset(baseC)
    sp.alloc("wosl", (128, 32 * 512), BF16)
    sp.alloc("orow", (128, 2 * 512), F32)
    sp.alloc("xrow", (128, D), F32)
    sp.alloc("hrow", (128, D), F32)
    sp.alloc("stat", (128, 8), F32)
    sp.alloc("x1row", (128, D), F32)
    sp.alloc("x1brow", (128, D), BF16)
    d_end = sp.mark()

    sp.reset(base0)
    sp.alloc("x1t", (128, H * P["N"]), BF16)
    sp.alloc("upsl", (128, 32 * 512), BF16)
    sp.alloc("zg", (128, 40), F32)
    sp.alloc("oh", (128, 8), F32)
    sp.alloc("zsel", (128, 32), BF16)
    sp.alloc("zselt", (32, NMAX), BF16)
    sp.alloc("erow", (128, 2 * 512), F32)
    sp.alloc("fxrow", (128, D), F32)
    sp.alloc("frrow", (128, D), F32)
    sp.alloc("fhrow", (128, D), F32)
    sp.alloc("fstat", (128, 8), F32)
    e_end = sp.mark()
    assert max(a_end, c_end, d_end, e_end) <= ARENA, (a_end, c_end, d_end, e_end)
    return sp.t



def emit_proj_pass(E, io, t, psA, v, which, groups, xT_in, dst, cos_in, sin_in, st):
    P = PAIR[v]
    W_in = io[{"q": "wq", "k": "wk", "v": "wv"}[which]]
    rot = which in ("q", "k")

    for (g0, glen) in groups:
        grb = glen // 128
        if st["xtg"]:
            E.emit("pl", lambda e: e.nop(), waits=[("pb", st["grp_rel"])])
        for K in range(NK):
            E.emit("pl",
                   (lambda K: lambda e: e.dma_start(
                       t["xt_g"][:, K * 1024: K * 1024 + glen],
                       xT_in[K * 128:(K + 1) * 128, g0:g0 + glen]))(K),
                   incs=[("xt", 16)])
        st["xtg"] += 1
        xt_target = E.cnt["xt"]

        if rot:
            if st["cosg"]:
                E.emit("sp", lambda e: e.nop(),
                       waits=[("pbd", st["cos_guard_d"]), ("t2d", st["cos_guard_t"])])
            for rb in range(grb):
                E.emit("sp",
                       (lambda rb: lambda e: e.dma_start(
                           t["cosT"][:, rb * 512:(rb + 1) * 512],
                           cos_in[g0 + rb * 128: g0 + (rb + 1) * 128, :]))(rb),
                       incs=[("cst", 16)])
                E.emit("sp",
                       (lambda rb: lambda e: e.dma_start(
                           t["sinT"][:, rb * 512:(rb + 1) * 512],
                           sin_in[g0 + rb * 128: g0 + (rb + 1) * 128, :]))(rb),
                       incs=[("cst", 16)])
            st["cosg"] += 1
        cst_target = E.cnt["cst"]

        for cp in range(_DEV_LIMIT["cps"]):
            for K in range(NK):
                slot = st["wseq"] % 8
                rel = st["slotrel"].get(slot)
                E.emit("pl",
                       (lambda K, cp, slot: lambda e: e.dma_start(
                           t["wsl"][:, slot * 512:(slot + 1) * 512],
                           W_in[K * 128:(K + 1) * 128, cp * 512:(cp + 1) * 512]))(K, cp, slot),
                       incs=[(f"wsl{slot}", 16)],
                       waits=[rel] if rel is not None else [])
                for rb in range(grb):
                    w2 = []
                    if rb == 0:
                        w2.append((f"wsl{slot}", (st["wseq"] // 8 + 1) * 16))
                        if K == 0:
                            w2.append(("xt", xt_target))
                    if K == 0:
                        rec = st["bank"].get(rb)
                        if rec is not None:
                            w2 += list(rec)
                    last_mm = (K == NK - 1)
                    last_rb = (rb == grb - 1)
                    incs = []
                    if last_mm:
                        incs = [("pb", 1)]
                    elif last_rb:
                        incs = [("wslf", 1)]
                    E.emit("pe",
                           (lambda K, rb, slot: lambda e: e.matmul(
                               psA[:, rb * 512:(rb + 1) * 512],
                               t["xt_g"][:, K * 1024 + rb * 128: K * 1024 + (rb + 1) * 128],
                               t["wsl"][:, slot * 512:(slot + 1) * 512],
                               start=(K == 0), stop=(K == NK - 1)))(K, rb, slot),
                           incs=incs,
                           waits=w2)
                if K == NK - 1:
                    st["slotrel"][slot] = ("pb", E.cnt["pb"])
                else:
                    st["slotrel"][slot] = ("wslf", E.cnt["wslf"])
                st["wseq"] += 1

            for rb in range(grb):
                pb_t = E.cnt["pb"] - (grb - 1 - rb)
                pslice = psA[:, rb * 512:(rb + 1) * 512]
                oslot = st["oseq"] % 4
                over = t["obuf"][:, oslot * 512:(oslot + 1) * 512]
                ob_guard = [(f"obw{oslot}", (st["oseq"] // 4) * 16)]
                if rot:
                    sset = st["oseq"] % 2
                    tmpv = t["ptmp"][:, sset * 512:(sset + 1) * 512]
                    rotv = t["rot"][:, sset * 512:(sset + 1) * 512]
                    t1v = t["t1"][:, sset * 512:(sset + 1) * 512]
                    t2v = t["t2"][:, sset * 512:(sset + 1) * 512]
                    cosv = t["cosT"][:, rb * 512:(rb + 1) * 512]
                    sinv = t["sinT"][:, rb * 512:(rb + 1) * 512]
                    tm3 = tmpv.rearrange("p (h x) -> p h x", h=4)
                    rot3 = rotv.rearrange("p (h x) -> p h x", h=4)
                    war = st["t1war"].get(sset, 0)
                    E.emit("act",
                           lambda e: e.copy(tmpv, pslice),
                           incs=[("pba", 1)],
                           waits=[("pb", pb_t), ("cst", cst_target),
                                  ("pbd", max(0, st["rotseq"] - 1)),
                                  ("t2d", max(0, st["rotseq"] - 1))])
                    E.emit("act",
                           lambda e: e.mul(rot3[:, :, 0:64], tm3[:, :, 64:128], -1.0),
                           waits=[("pba", E.cnt["pba"])])
                    E.emit("act",
                           lambda e: e.copy(rot3[:, :, 64:128], tm3[:, :, 0:64]),
                           incs=[("rotc", 1)])
                    E.emit("dve",
                           lambda e: e.tensor_tensor(t1v, tmpv, cosv, ALU.mult),
                           incs=[("pbd", 1)],
                           waits=[("pba", E.cnt["pba"]), ("ob", war)])
                    E.emit("dve",
                           lambda e: e.tensor_tensor(t2v, rotv, sinv, ALU.mult),
                           incs=[("t2d", 1)],
                           waits=[("rotc", E.cnt["rotc"]), ("ob", war)])
                    E.emit("dve",
                           lambda e: e.tensor_tensor(over, t1v, t2v, ALU.add),
                           incs=[("ob", 1)],
                           waits=[("t2d", E.cnt["t2d"]), ("pbd", E.cnt["pbd"])] + ob_guard)
                    st["t1war"][sset] = E.cnt["ob"]
                    st["rotseq"] += 1
                    st["bank"][rb] = [("pba", E.cnt["pba"])]
                else:
                    E.emit("act",
                           lambda e: e.copy(over, pslice),
                           incs=[("obv", 1)],
                           waits=[("pb", pb_t)] + ob_guard)
                    st["bank"][rb] = [("obv", E.cnt["obv"])]
                r0 = g0 + rb * 128
                if st["kvdyn"]:
                    r0 += (E.parity or 0) * st["kvr"]
                obsem = "ob" if rot else "obv"
                E.emit("sp",
                       (lambda r0, cp, over: lambda e: e.dma_start(
                           dst[r0:r0 + 128, cp * 512:(cp + 1) * 512], over))(r0, cp, over),
                       incs=[(f"obw{oslot}", 16)],
                       waits=[(obsem, E.cnt[obsem])])
                st["oseq"] += 1
        st["grp_rel"] = E.cnt["pb"]
        st["cos_guard_d"] = E.cnt["pbd"]
        st["cos_guard_t"] = E.cnt["t2d"]


def emit_phase_a(E, io, t, psA, v, kvoff):
    P = PAIR[v]
    st = dict(wseq=0, oseq=0, rotseq=0, xtg=0, cosg=0, grp_rel=0,
              cos_guard_d=0, cos_guard_t=0, bank={}, t1war={}, slotrel={},
              kvdyn=True, kvr=PAIR[v]["KVR"])
    for which, dst, cos_in, sin_in in (
        ("k", io["k_tok"], io["cos_kv"], io["sin_kv"]),
        ("v", io["v_tok"], None, None),
    ):
        if which not in _DEV_LIMIT["passes"]:
            continue
        emit_proj_pass(E, io, t, psA, v, which, P["KVG"][: _DEV_LIMIT["maxgrp"]],
                       io["xT_kv"], dst, cos_in, sin_in, st)
    E.emit("sp", lambda e: e.nop(), incs=[("kvw", 1)],
           waits=[(f"obw{i}", E.cnt[f"obw{i}"]) for i in range(4)])
    st["kvdyn"] = False
    if "q" in _DEV_LIMIT["passes"]:
        emit_proj_pass(E, io, t, psA, v, "q", P["QG"][: _DEV_LIMIT["maxgrp"]],
                       io["xT_q"], io["q_tok"], io["cos_q"], io["sin_q"], st)
    E.emit("sp", lambda e: e.nop(), incs=[("kvw", 1)],
           waits=[(f"obw{i}", E.cnt[f"obw{i}"]) for i in range(4)])



def emit_barrier(E, io, t, v):
    E.emit("pl", lambda e: e.dma_start(t["tok_sb"][:], io["token"][:]),
           incs=[("flg", 16)])

    if _DEV_PARTS["bar_wflag"]:
        def wflag(e):
            pid2 = E.nc.partition_id(engines=[e.engine])
            slot = pid2 % 2
            return e.dma_start(io["flags"][ds(slot, 1), :], t["tok_sb"][:])
        E.emit("pl", wflag, incs=[("flg", 16)],
               waits=[("kvw", 1), ("flg", 16)])

    def poll(e):
        tok = e.alloc_register(f"tok{v}")
        e.load(tok, t["tok_sb"][0:1, 0:1])
        for slot in range(2):
            val = e.alloc_register(f"fv{v}_{slot}")
            e.load(val, io["flags"][slot:slot + 1, 0:1])
            for i in range(120):
                cyc = 2400 if i < 40 else 30000
                with e.If_ne(val, tok):
                    if _DEV_LIMIT.get("simnop"):
                        e.nop()
                    else:
                        e.nop(cycle_cnt=cyc)
                    e.load(val, io["flags"][slot:slot + 1, 0:1])
        return e.nop()
    if _DEV_PARTS["bar_poll"]:
        E.emit("pl", poll, incs=[("bar", 1)], waits=[("flg", E.cnt["flg"])])
    else:
        E.emit("pl", lambda e: e.nop(), incs=[("flg", 1)], waits=[("flg", E.cnt["flg"])])


def emit_fence(E, sems):
    for owner in ENGS:
        E.emit(owner, lambda e: e.nop(),
               waits=[(s, E.cnt[s]) for s in sems if E.cnt[s] > 0])


ENGS = ("pe", "act", "dve", "pl", "sp")



def emit_phase_c(E, io, t, psA, v):
    P = PAIR[v]
    N, KVL = P["N"], P["KVL"]
    NKB = KVL // 128
    psc = [psA[:, 0:1024], psA[:, 1024:2048]]
    psz = psA[:, 2048:2560]
    psx = psA[:, 2560:3072]

    emit_fence(E, ["kvw", "bar"] + [f"obw{i}" for i in range(4)])

    E.emit("pl", lambda e: e.dma_start(
        t["mskA"][:].rearrange("p (j c) -> p j c", c=512),
        io["dmaskA"].rearrange("j p c -> p j c")), incs=[("mld", 16)])
    E.emit("pl", lambda e: e.dma_start(
        t["mskB"][:].rearrange("p (j c) -> p j c", c=256),
        io["dmaskB"].rearrange("j p c -> p j c")), incs=[("mld", 16)])
    mld_t = E.cnt["mld"]

    st = dict(hrel={}, reg_rel=[0, 0], useq=0, zrel=0, xrel=0, rzrel={})
    pend = None

    def emit_tail(p):
        for kb in range(p["nkb"]):
            w2 = []
            if kb == 0:
                w2 = [("expd", p["expd_t"]), ("mskd", p["mskd_t"]),
                      ("rzd", st["zrel"])]
            E.emit("pe",
                   (lambda kb: lambda e: e.matmul(
                       psz[:, 0:p["W"]], t["ones128"][:],
                       p["expt"][:, kb * 512: kb * 512 + p["W"]],
                       start=(kb == 0), stop=(kb == p["nkb"] - 1)))(kb),
                   incs=[("sc", 1)] if kb == p["nkb"] - 1 else [],
                   waits=w2)
        zmm_t = E.cnt["sc"]
        rzwar = st["rzrel"].get(p["ub"], 0)
        E.emit("act",
               lambda e: e.activation(p["zlnv"], psz[:, 0:p["W"]], AF.Ln),
               incs=[("rzd", 1)],
               waits=[("sc", zmm_t), ("ctxd", rzwar)])
        st["zrel"] = E.cnt["rzd"]
        E.emit("act",
               lambda e: e.activation(p["rzbv"], p["zlnv"], AF.Exp, scale=-1.0),
               incs=[("rzd", 1)],
               waits=[("rzd", E.cnt["rzd"])])
        rzd_t = E.cnt["rzd"]
        for kb in range(p["nkb"]):
            w2 = [(f"vld{p['hb']}", p["vld_t"]), ("ctxd", st["xrel"])] if kb == 0 else []
            E.emit("pe",
                   (lambda kb: lambda e: e.matmul(
                       psx[:, 0:p["W"]],
                       t["vt"][:, p["hb"] * KVL + kb * 128: p["hb"] * KVL + (kb + 1) * 128],
                       p["expt"][:, kb * 512: kb * 512 + p["W"]],
                       start=(kb == 0), stop=(kb == p["nkb"] - 1)))(kb),
                   incs=[("pvd", 1)] if kb == p["nkb"] - 1 else [],
                   waits=w2)
        pvd_t = E.cnt["pvd"]
        E.emit("dve",
               lambda e: e.tensor_tensor(
                   t["ctx"][:, p["h"] * N + p["qc0"]: p["h"] * N + p["qc0"] + p["W"]],
                   psx[:, 0:p["W"]], p["rzbv"], ALU.mult),
               incs=[("ctxd", 1)],
               waits=[("pvd", pvd_t), ("rzd", rzd_t)])
        st["xrel"] = E.cnt["ctxd"]
        st["rzrel"][p["ub"]] = E.cnt["ctxd"]
        st["exprel_%d" % p["ub"]] = E.cnt["pvd"]
        if p["last_of_head"]:
            st["hrel"][p["hb"]] = ("pvd", E.cnt["pvd"])

    for h in range(H):
        hb = h % 2
        rel = st["hrel"].get(hb)
        ktv = t["kt"][:, hb * KVL:(hb + 1) * KVL]
        vtv = t["vt"][:, hb * KVL:(hb + 1) * KVL]
        qtv = t["qt"][:, hb * N:(hb + 1) * N]
        E.emit("sp",
               (lambda h, ktv: lambda e: e.dma_start(
                   ktv, io["k_tok"][0:KVL, h * 128:(h + 1) * 128],
                   transpose=True))(h, ktv),
               incs=[(f"kld{hb}", 16)],
               waits=[rel] if rel is not None else [])
        nsplit = 4
        kbp = NKB // nsplit
        for s4 in range(nsplit):
            E.emit("sp",
                   (lambda h, s4, vtv: lambda e: e.dma_start(
                       vtv.rearrange("p (kb d) -> p kb d", d=128)[:, s4 * kbp:(s4 + 1) * kbp, :],
                       io["v_tok"].rearrange("(kb p) d -> p kb d", p=128)[
                           :, s4 * kbp:(s4 + 1) * kbp, h * 128:(h + 1) * 128]))(h, s4, vtv),
                   incs=[(f"vld{hb}", 16)])
        E.emit("sp",
               (lambda h, qtv: lambda e: e.dma_start(
                   qtv, io["q_tok"][0:N, h * 128:(h + 1) * 128],
                   transpose=True))(h, qtv),
               incs=[(f"qld{hb}", 16)])
        kld_t, vld_t, qld_t = E.cnt[f"kld{hb}"], E.cnt[f"vld{hb}"], E.cnt[f"qld{hb}"]

        for qci, (qc0, W, nkb) in enumerate(P["QCS"]):
            u = st["useq"]
            ub = u % 2
            nkbmax = P["QCS"][-1][2]
            expt = t["expt"][:, ub * nkbmax * 512:(ub + 1) * nkbmax * 512]
            for kbg in range((nkb + 1) // 2):
                reg = kbg % 2
                for j in (0, 1):
                    kb = kbg * 2 + j
                    if kb >= nkb:
                        continue
                    w2 = []
                    if j == 0:
                        w2.append(("expd", st["reg_rel"][reg]))
                    if kbg == 0 and j == 0:
                        w2 += [(f"kld{hb}", kld_t), (f"qld{hb}", qld_t),
                               ("mld", mld_t), ("pvd", st.get("exprel_%d" % ub, 0))]
                    E.emit("pe",
                           (lambda kb, j, reg, qc0, W: lambda e: e.matmul(
                               psc[reg][:, j * 512: j * 512 + W],
                               t["kt"][:, hb * KVL + kb * 128: hb * KVL + (kb + 1) * 128],
                               t["qt"][:, hb * N + qc0: hb * N + qc0 + W],
                               start=True, stop=True))(kb, j, reg, qc0, W),
                           incs=[("sc", 1)],
                           waits=w2)
                nj = min(2, nkb - kbg * 2)
                E.emit("act",
                       (lambda kbg, reg, nj, expt: lambda e: e.activation(
                           expt[:, kbg * 1024: kbg * 1024 + nj * 512],
                           psc[reg][:, 0: nj * 512],
                           AF.Exp, scale=ISQ))(kbg, reg, nj, expt),
                       incs=[("expd", 1)],
                       waits=[("sc", E.cnt["sc"])])
                st["reg_rel"][reg] = E.cnt["expd"]
            expd_t = E.cnt["expd"]
            dkb = 8 if W == 512 else 4
            mskt = t["mskA"] if W == 512 else t["mskB"]
            mw = 512 if W == 512 else 256
            for j in range(dkb):
                kb = nkb - dkb + j
                E.emit("dve",
                       (lambda kb, j, expt, W, mw: lambda e: e.tensor_tensor(
                           expt[:, kb * 512: kb * 512 + W],
                           expt[:, kb * 512: kb * 512 + W],
                           mskt[:, j * mw: j * mw + W], ALU.mult))(kb, j, expt, W, mw),
                       incs=[("mskd", 1)],
                       waits=[("expd", expd_t)] if j == 0 else [])
            mskd_t = E.cnt["mskd"]
            newp = dict(h=h, hb=hb, ub=ub, qc0=qc0, W=W, nkb=nkb,
                        expt=expt, expd_t=expd_t, mskd_t=mskd_t,
                        vld_t=vld_t,
                        zlnv=t["zln"][:, ub * 512: ub * 512 + W],
                        rzbv=t["rzb"][:, ub * 512: ub * 512 + W],
                        last_of_head=(qci == len(P["QCS"]) - 1))
            if pend is not None:
                emit_tail(pend)
            pend = newp
            st["useq"] += 1
    if pend is not None:
        emit_tail(pend)



def ln_pass(E, t, xa, xb_, out_f32, out_bf, stat, sq):
    E.emit("dve", lambda e: e.tensor_tensor(xb_, xa, xb_, ALU.add),
           incs=[("lnd", 1)])
    E.emit("dve", lambda e: e.reduce_sum(stat[:, 0:1], xb_, axis=AX.X),
           incs=[("lnd", 1)], waits=[("lnd", E.cnt["lnd"])])
    E.emit("dve", lambda e: e.tensor_scalar_mul(stat[:, 1:2], stat[:, 0:1],
                                                -1.0 / D),
           incs=[("lnd", 1)], waits=[("lnd", E.cnt["lnd"])])
    E.emit("act", lambda e: e.activation(sq, xb_, AF.Square,
                                         bias=stat[:, 1:2]),
           incs=[("lna", 1)], waits=[("lnd", E.cnt["lnd"])])
    E.emit("dve", lambda e: e.reduce_sum(stat[:, 2:3], sq, axis=AX.X),
           incs=[("lnd", 1)], waits=[("lna", E.cnt["lna"])])
    E.emit("dve", lambda e: e.memset(stat[:, 6:7], EPS), incs=[("lnd", 1)])
    E.emit("act", lambda e: e.activation(stat[:, 3:4], stat[:, 2:3], AF.Sqrt,
                                         scale=1.0 / D, bias=stat[:, 6:7]),
           incs=[("lna", 1)], waits=[("lnd", E.cnt["lnd"])])
    E.emit("dve", lambda e: e.reciprocal(stat[:, 4:5], stat[:, 3:4]),
           incs=[("lnd", 1)], waits=[("lna", E.cnt["lna"])])
    E.emit("dve", lambda e: e.tensor_tensor(stat[:, 5:6], stat[:, 1:2],
                                            stat[:, 4:5], ALU.mult),
           incs=[("lnd", 1)], waits=[("lnd", E.cnt["lnd"])])
    E.emit("dve", lambda e: e.tensor_scalar(out_f32, xb_, stat[:, 4:5],
                                            stat[:, 5:6], ALU.mult, ALU.add),
           incs=[("lnd", 1)], waits=[("lnd", E.cnt["lnd"])])
    if out_bf is not None:
        E.emit("act", lambda e: e.copy(out_bf, out_f32),
               incs=[("lna", 1)], waits=[("lnd", E.cnt["lnd"])])


def emit_phase_d(E, io, t, psA, v):
    P = PAIR[v]
    N = P["N"]
    NRB = N // 128
    emit_fence(E, ["ctxd", "pvd", "expd", "rzd", "mskd"])
    rowsets = [list(range(8)), list(range(8, NRB))] if NRB > 8 else [list(range(NRB))]
    worel = {}
    for rs in rowsets:
        if not rs:
            continue
        for cc in range(NHC):
            for half in (0, 1):
                wsem = f"wo{half}"
                rel = worel.get(half)
                for s4 in range(4):
                    E.emit("pl",
                           (lambda cc, half, s4: lambda e: e.dma_start(
                               t["wosl"][:, half * 8192 + s4 * 2048:
                                         half * 8192 + (s4 + 1) * 2048],
                               io["wo"].rearrange("(kk p) c -> p kk c", p=128)[
                                   :, half * 16 + s4 * 4: half * 16 + (s4 + 1) * 4,
                                   cc * 512:(cc + 1) * 512]))(cc, half, s4),
                           incs=[(wsem, 16)],
                           waits=[rel] if (rel is not None and s4 == 0) else [])
                wo_t = E.cnt[wsem]
                for bi, rb in enumerate(rs):
                    for hh in range(16):
                        h = half * 16 + hh
                        w2 = []
                        if hh == 0 and bi == 0:
                            w2.append((wsem, wo_t))
                        if hh == 0 and half == 0:
                            rec = E.dbank.get(bi)
                            if rec is not None:
                                w2.append(rec)
                        last = (h == H - 1)
                        E.emit("pe",
                               (lambda h, hh, bi, rb, cc: lambda e: e.matmul(
                                   psA[:, bi * 512:(bi + 1) * 512],
                                   t["ctx"][:, h * N + rb * 128: h * N + (rb + 1) * 128],
                                   t["wosl"][:, (h % 16) * 512 + (h // 16) * 8192:
                                             (h % 16) * 512 + (h // 16) * 8192 + 512],
                                   start=(h == 0), stop=(h == H - 1)))(h, hh, bi, rb, cc),
                               incs=[("pb", 1)] if last else (
                                   [("wod", 1)] if (hh == 15 and bi == len(rs) - 1) else []),
                               waits=w2)
                if half == 0:
                    worel[0] = ("wod", E.cnt["wod"])
                else:
                    worel[1] = ("pb", E.cnt["pb"])
            for bi, rb in enumerate(rs):
                pb_t = E.cnt["pb"] - (len(rs) - 1 - bi)
                po = (cc * len(rs) + bi) % 2
                orv = t["orow"][:, po * 512:(po + 1) * 512]
                E.emit("dve",
                       (lambda bi, orv: lambda e: e.tensor_copy(
                           orv, psA[:, bi * 512:(bi + 1) * 512]))(bi, orv),
                       incs=[("doe", 1)],
                       waits=[("pb", pb_t),
                              (f"dw{po}", E.ow.get(po, 0) * 16)])
                E.emit("act",
                       (lambda rb, cc, orv: lambda e: e.dma_start(
                           io["attn_s"][rb * 128:(rb + 1) * 128,
                                        cc * 512:(cc + 1) * 512], orv))(rb, cc, orv),
                       incs=[(f"dw{po}", 16)],
                       waits=[("doe", E.cnt["doe"])])
                E.ow[po] = E.ow.get(po, 0) + 1
                E.dbank[bi] = ("doe", E.cnt["doe"])
    emit_fence(E, ["dw0", "dw1", "doe"])
    for rb in range(NRB):
        g = [("lnd", E.cnt["lnd"]), ("lna", E.cnt["lna"]),
             ("x1w", E.cnt["x1w"]), ("x1bw", E.cnt["x1bw"])]
        E.emit("act",
               (lambda rb: lambda e: e.dma_start(
                   t["xrow"][:], io["x_own"][rb * 128:(rb + 1) * 128, :]))(rb),
               incs=[("dw0", 16)], waits=g)
        E.emit("act",
               (lambda rb: lambda e: e.dma_start(
                   t["hrow"][:], io["attn_s"][rb * 128:(rb + 1) * 128, :]))(rb),
               incs=[("dw0", 16)])
        E.emit("dve", lambda e: e.nop(), waits=[("dw0", E.cnt["dw0"])] + g)
        ln_pass(E, t, t["xrow"][:], t["hrow"][:], t["x1row"][:],
                t["x1brow"][:], t["stat"][:], t["xrow"][:])
        E.emit("act",
               (lambda rb: lambda e: e.dma_start(
                   io["x1_s"][rb * 128:(rb + 1) * 128, :], t["x1row"][:]))(rb),
               incs=[("x1w", 16)],
               waits=[("lnd", E.cnt["lnd"])])
        E.emit("sp",
               (lambda rb: lambda e: e.dma_start(
                   io["x1b_s"][rb * 128:(rb + 1) * 128, :], t["x1brow"][:]))(rb),
               incs=[("x1bw", 16)],
               waits=[("lna", E.cnt["lna"])])



def emit_phase_e(E, io, t, psA, v):
    P = PAIR[v]
    N = P["N"]
    NRB = N // 128
    emit_fence(E, ["x1w", "x1bw", "lnd", "lna"])
    E.emit("pl", lambda e: e.dma_start(
        t["ag_sb"][:].rearrange("p (kk c) -> p kk c", c=36),
        io["ag"].rearrange("(kk p) c -> p kk c", p=128)), incs=[("mld", 16)])
    E.emit("pl", lambda e: e.dma_start(t["bias_sb"][:], io["bias128"][:]),
           incs=[("mld", 16)])
    E.emit("pl", lambda e: e.dma_start(t["lorab_sb"][:], io["lorabf"][:]),
           incs=[("mld", 16)])
    E.emit("pl", lambda e: e.dma_start(t["id_sb"][:], io["id128"][:]),
           incs=[("mld", 16)])
    mld_t = E.cnt["mld"]
    for kk in range(NK):
        E.emit("sp",
               (lambda kk: lambda e: e.dma_start(
                   t["x1t"][:, kk * N:(kk + 1) * N],
                   io["x1b_s"][0:N, kk * 128:(kk + 1) * 128],
                   transpose=True))(kk),
               incs=[("x1t", 16)])
    x1t_t = E.cnt["x1t"]

    psz = psA[:, 3072:3584]
    pst_h = None
    for rb in range(NRB):
        for kk in range(NK):
            w2 = []
            if kk == 0:
                w2 = [("x1t", x1t_t), ("mld", mld_t), ("zgd", E.cnt["zgd"])]
            E.emit("pe",
                   (lambda rb, kk: lambda e: e.matmul(
                       psz[:, 0:36],
                       t["x1t"][:, kk * N + rb * 128: kk * N + (rb + 1) * 128],
                       t["ag_sb"][:, kk * 36:(kk + 1) * 36],
                       start=(kk == 0), stop=(kk == NK - 1)))(rb, kk),
                   incs=[("sc", 1)] if kk == NK - 1 else [],
                   waits=w2)
        E.emit("dve", lambda e: e.tensor_copy(t["zg"][:, 0:36], psz[:, 0:36]),
               incs=[("zgd", 1)], waits=[("sc", E.cnt["sc"]),
                                         ("ohd", E.cnt["ohd"])])
        zg = t["zg"]
        oh = t["oh"]
        def dv(fn):
            E.emit("dve", fn, incs=[("ohd", 1)], waits=[("ohd", E.cnt["ohd"])])
        E.emit("dve", lambda e: e.tensor_tensor(
            zg[:, 32:36], zg[:, 32:36], t["bias_sb"][:], ALU.add),
            incs=[("ohd", 1)], waits=[("zgd", E.cnt["zgd"])])
        l = [zg[:, 32 + i: 33 + i] for i in range(4)]
        dv(lambda e: e.tensor_tensor(oh[:, 4:5], l[0], l[1], ALU.max))
        dv(lambda e: e.tensor_tensor(oh[:, 5:6], l[2], l[3], ALU.max))
        dv(lambda e: e.tensor_tensor(oh[:, 6:7], oh[:, 4:5], oh[:, 5:6], ALU.max))
        m = oh[:, 6:7]
        dv(lambda e: e.tensor_tensor(oh[:, 0:1], l[0], m, ALU.is_ge))
        dv(lambda e: e.tensor_tensor(oh[:, 1:2], l[1], m, ALU.is_ge))
        dv(lambda e: e.tensor_tensor(oh[:, 2:3], l[2], m, ALU.is_ge))
        dv(lambda e: e.tensor_tensor(oh[:, 3:4], l[3], m, ALU.is_ge))
        dv(lambda e: e.tensor_tensor(oh[:, 4:5], oh[:, 1:2], oh[:, 0:1], ALU.mult))
        dv(lambda e: e.tensor_tensor(oh[:, 1:2], oh[:, 1:2], oh[:, 4:5], ALU.subtract))
        dv(lambda e: e.tensor_tensor(oh[:, 4:5], oh[:, 0:1], oh[:, 1:2], ALU.add))
        dv(lambda e: e.tensor_tensor(oh[:, 5:6], oh[:, 2:3], oh[:, 4:5], ALU.mult))
        dv(lambda e: e.tensor_tensor(oh[:, 2:3], oh[:, 2:3], oh[:, 5:6], ALU.subtract))
        dv(lambda e: e.tensor_tensor(oh[:, 4:5], oh[:, 4:5], oh[:, 2:3], ALU.add))
        dv(lambda e: e.tensor_tensor(oh[:, 5:6], oh[:, 3:4], oh[:, 4:5], ALU.mult))
        dv(lambda e: e.tensor_tensor(oh[:, 3:4], oh[:, 3:4], oh[:, 5:6], ALU.subtract))
        for ei in range(4):
            dv((lambda ei: lambda e: e.tensor_scalar_mul(
                t["zsel"][:, ei * 8:(ei + 1) * 8], zg[:, ei * 8:(ei + 1) * 8],
                oh[:, ei: ei + 1]))(ei))
        E.emit("pe",
               (lambda rb: lambda e: e.transpose(
                   E.psT[0:32, 7168:7296], t["zsel"][:, 0:32],
                   t["id_sb"][:]))(rb),
               incs=[("sc", 1)],
               waits=[("ohd", E.cnt["ohd"]), ("eoe", E.cnt["eoe"])])
        E.emit("dve",
               (lambda rb: lambda e: e.tensor_copy(
                   t["zselt"][:, rb * 128:(rb + 1) * 128],
                   E.psT[0:32, 7168:7296]))(rb),
               incs=[("eoe", 1)], waits=[("sc", E.cnt["sc"])])
    zsel_t = E.cnt["eoe"]

    rowsets = [list(range(8)), list(range(8, NRB))] if NRB > 8 else [list(range(NRB))]
    uprel = {}
    ebank = {}
    for rs in rowsets:
        if not rs:
            continue
        for cc in range(NHC):
            usem = "up0"
            rel = uprel.get(0)
            for s8 in range(8):
                E.emit("pl",
                       (lambda cc, s8: lambda e: e.dma_start(
                           t["upsl"][:, s8 * 2048:(s8 + 1) * 2048],
                           io["wup"].rearrange("(kk p) c -> p kk c", p=128)[
                               :, s8 * 4:(s8 + 1) * 4,
                               cc * 512:(cc + 1) * 512]))(cc, s8),
                       incs=[(usem, 16)],
                       waits=[rel] if (rel is not None and s8 == 0) else [])
            up_t = E.cnt[usem]
            for bi, rb in enumerate(rs):
                for kk in range(NK):
                    w2 = []
                    if kk == 0:
                        if bi == 0:
                            w2.append((usem, up_t))
                        w2.append(("eoe", zsel_t))
                        rec = ebank.get(bi)
                        if rec is not None:
                            w2.append(rec)
                    E.emit("pe",
                           (lambda rb, kk, bi: lambda e: e.matmul(
                               psA[:, bi * 512:(bi + 1) * 512],
                               t["x1t"][:, kk * N + rb * 128: kk * N + (rb + 1) * 128],
                               t["upsl"][:, kk * 512:(kk + 1) * 512],
                               start=(kk == 0), stop=False))(rb, kk, bi),
                           waits=w2)
                last_up = (bi == len(rs) - 1)
                E.emit("pe",
                       (lambda rb, cc, bi: lambda e: e.matmul(
                           psA[:, bi * 512:(bi + 1) * 512],
                           t["zselt"][:, rb * 128:(rb + 1) * 128],
                           t["lorab_sb"][:, cc * 512:(cc + 1) * 512],
                           start=False, stop=True))(rb, cc, bi),
                       incs=[("pb", 1)] if not last_up else [("upd", 1)])
                if last_up:
                    uprel[0] = ("upd", E.cnt["upd"])
            for bi, rb in enumerate(rs):
                if bi == len(rs) - 1:
                    pb_t = ("upd", E.cnt["upd"])
                else:
                    pb_t = ("pb", E.cnt["pb"] - (len(rs) - 2 - bi))
                po = (cc * len(rs) + bi) % 2
                erv = t["erow"][:, po * 512:(po + 1) * 512]
                E.emit("dve",
                       (lambda bi, erv: lambda e: e.tensor_copy(
                           erv, psA[:, bi * 512:(bi + 1) * 512]))(bi, erv),
                       incs=[("eoe", 1)],
                       waits=[pb_t, (f"ew{po}", E.ew.get(po, 0) * 16)])
                E.emit("act",
                       (lambda rb, cc, erv: lambda e: e.dma_start(
                           io["ffn_s"][rb * 128:(rb + 1) * 128,
                                       cc * 512:(cc + 1) * 512], erv))(rb, cc, erv),
                       incs=[(f"ew{po}", 16)],
                       waits=[("eoe", E.cnt["eoe"])])
                E.ew[po] = E.ew.get(po, 0) + 1
                ebank[bi] = ("eoe", E.cnt["eoe"])
    emit_fence(E, ["ew0", "ew1", "eoe"])
    for rb in range(NRB):
        g = [("lnd", E.cnt["lnd"]), ("lna", E.cnt["lna"]),
             ("yw", E.cnt["yw"])]
        E.emit("act",
               (lambda rb: lambda e: e.dma_start(
                   t["fxrow"][:], io["x1_s"][rb * 128:(rb + 1) * 128, :]))(rb),
               incs=[("ffw", 16)], waits=g)
        E.emit("act",
               (lambda rb: lambda e: e.dma_start(
                   t["frrow"][:], io["ffn_s"][rb * 128:(rb + 1) * 128, :]))(rb),
               incs=[("ffw", 16)])
        E.emit("dve", lambda e: e.nop(), waits=[("ffw", E.cnt["ffw"])] + g)
        ln_pass(E, t, t["fxrow"][:], t["frrow"][:], t["fhrow"][:], None,
                t["fstat"][:], t["fxrow"][:])
        E.emit("act",
               (lambda rb: lambda e: e.dma_start(
                   io["y"][rb * 128:(rb + 1) * 128, :], t["fhrow"][:]))(rb),
               incs=[("yw", 16)],
               waits=[("lnd", E.cnt["lnd"])])
    E.emit("act", lambda e: e.nop(), waits=[("yw", E.cnt["yw"])])


def emit_variant(nc, e, engname, v, io, t, psA, sems, parity=None, psT=None):
    E = Ectx(nc, e, engname, sems, parity=parity)
    E.psT = psT
    E.dbank = {}
    E.ow = {}
    E.ew = {}
    if _DEV_PARTS["memset"]:
        E.emit("pl", lambda e: e.memset(t["ones128"][:], 1.0), incs=[("plc", 1)])
    if _DEV_PARTS["phase_a"]:
        emit_phase_a(E, io, t, psA, v, kvoff=0)
    if _DEV_PARTS["barrier"]:
        emit_barrier(E, io, t, v)
    if _DEV_PARTS.get("phase_c", True):
        emit_phase_c(E, io, t, psA, v)
    if _DEV_PARTS.get("phase_d", True):
        emit_phase_d(E, io, t, psA, v)
    if _DEV_PARTS.get("phase_e", True):
        emit_phase_e(E, io, t, psA, v)
    return E



_PROG_CACHE = {}


def build_program(debug=False):
    key = bool(debug)
    if key in _PROG_CACHE:
        return _PROG_CACHE[key]
    nc = bass.Bass("TRN2", target_bir_lowering=False, num_devices=8)
    io = declare_io(nc, debug)
    arena = nc.alloc_sbuf_tensor("arena", [128, ARENA], U8)
    tens = [plan_sbuf(arena, v) for v in (0, 1)]
    psA_h = nc.alloc_psum_tensor("psA", [128, 4096], F32)
    psA = psA_h.ap()
    psT = psA_h.bitcast(BF16).ap()
    with contextlib.ExitStack() as stack:
        sems = {n: stack.enter_context(nc.semaphore(n)) for n in SEM_NAMES}
        with nc.Block() as block:
            def mk(engname):
                def body(e):
                    pid = nc.partition_id(engines=[e.engine])
                    vr = e.alloc_register("vr")
                    if engname == "sp":
                        e.reg_alu(vr, pid, 4, ALU.mod)
                        vv = e.snap(vr, min_val=0, max_val=3)
                        for idx in e.Switch(vv, 4):
                            emit_variant(nc, e, engname, idx // 2, io,
                                         tens[idx // 2], psA, sems,
                                         parity=idx % 2, psT=psT)
                    else:
                        e.reg_alu(vr, pid, 4, ALU.mod)
                        e.reg_alu(vr, vr, 2, ALU.divide)
                        vv = e.snap(vr, min_val=0, max_val=1)
                        for v in e.Switch(vv, 2):
                            emit_variant(nc, e, engname, v, io, tens[v],
                                         psA, sems, psT=psT)
                return body
            block.tensor(mk("pe"))
            block.scalar(mk("act"))
            block.vector(mk("dve"))
            block.gpsimd(mk("pl"))
            block.sync(mk("sp"))
    _PROG_CACHE[key] = (nc, io)
    return nc, io



_TOKEN_COUNTER = [12345]


def stage_core(inputs, c):
    b, ri = c // 4, c % 4
    v, parity = ri // 2, ri % 2
    P = PAIR[v]
    x = np.asarray(inputs["L0_attn_hidden_states"])
    W0, W1 = (0, A2) if v == 0 else (A2, S)
    own = np.arange(W0 + parity, W1, 2)
    kv0 = (0, P["KVR"]) if parity == 0 else (P["KVR"], P["KVL"])
    kvrows = np.arange(kv0[0], kv0[1])

    xb = x[b]
    x_own = np.zeros((NMAX, D), np.float32)
    x_own[: P["N"]] = xb[own]
    xT_q = np.zeros((D, NMAX), np.float32)
    xT_q[:, : P["N"]] = xb[own].T
    xT_kv = np.zeros((D, KVRMAX), np.float32)
    xT_kv[:, : P["KVR"]] = xb[kvrows].T

    inv = (1.0 / 10000.0 ** (np.arange(0, HD, 2) / HD)).astype(np.float32)

    def costabs(pos):
        emb = pos[:, None].astype(np.float32) * inv[None, :]
        emb = np.concatenate([emb, emb], axis=1)
        return (np.tile(np.cos(emb), (1, 4)).astype(np.float32),
                np.tile(np.sin(emb), (1, 4)).astype(np.float32))

    cq = np.zeros((NMAX, 512), np.float32)
    sq = np.zeros((NMAX, 512), np.float32)
    cq[: P["N"]], sq[: P["N"]] = costabs(own)
    ckv = np.zeros((KVRMAX, 512), np.float32)
    skv = np.zeros((KVRMAX, 512), np.float32)
    ckv[: P["KVR"]], skv[: P["KVR"]] = costabs(kvrows)

    mA = np.zeros((8, 128, 512), np.float32)
    for j in range(8):
        ki = np.arange(128)[:, None] + j * 128
        qi = np.arange(512)[None, :] * 2 + parity
        mA[j] = (ki <= qi).astype(np.float32)
    mB = np.zeros((4, 128, 256), np.float32)
    for j in range(4):
        ki = np.arange(128)[:, None] + j * 128
        qi = np.arange(256)[None, :] * 2 + parity
        mB[j] = (ki <= qi).astype(np.float32)

    la = np.asarray(inputs["L0_ffn_lora_A"]).reshape(32, D)
    ag = np.ascontiguousarray(
        np.concatenate([la.T, np.asarray(inputs["L0_ffn_gate_w"]).T], axis=1),
        dtype=np.float32)
    bias128 = np.tile(np.asarray(inputs["L0_ffn_expert_bias"]).reshape(1, 4),
                      (128, 1)).astype(np.float32)
    lorabf = np.ascontiguousarray(
        np.asarray(inputs["L0_ffn_lora_B"]).transpose(0, 2, 1).reshape(32, D),
        dtype=np.float32)

    cfg = np.zeros((1, 16), np.int32)
    cfg[0, 0] = kv0[0]
    cfg[0, 1] = parity

    return dict(
        x_own=x_own, xT_q=np.ascontiguousarray(xT_q),
        xT_kv=np.ascontiguousarray(xT_kv),
        cos_q=cq, sin_q=sq, cos_kv=ckv, sin_kv=skv,
        wq=np.ascontiguousarray(inputs["L0_attn_wq"], np.float32),
        wk=np.ascontiguousarray(inputs["L0_attn_wk"], np.float32),
        wv=np.ascontiguousarray(inputs["L0_attn_wv"], np.float32),
        wo=np.ascontiguousarray(inputs["L0_attn_wo"], np.float32),
        wup=np.ascontiguousarray(np.asarray(inputs["L0_ffn_W_up"], np.float32).T),
        ag=ag, bias128=bias128, lorabf=lorabf,
        id128=np.eye(128, dtype=np.float32),
        dmaskA=mA,
        dmaskB=mB,
        cfg=cfg,
    ), own, b


def kernel(**inputs):
    nc, io = build_program(False)
    _TOKEN_COUNTER[0] += 1
    token = np.full((1, 16), (_TOKEN_COUNTER[0] * 2654435761) & 0x7FFFFFFF,
                    dtype=np.int32)
    in_maps = []
    metas = []
    for c in range(8):
        m, own, b = stage_core(inputs, c)
        m["token"] = token
        in_maps.append(m)
        metas.append((own, b))
    res = run_bass_kernel_spmd(nc, in_maps, core_ids=list(range(8)))
    out = np.empty((2, S, D), np.float32)
    for c in range(8):
        own, b = metas[c]
        P = PAIR[(c % 4) // 2]
        out[b, own, :] = res.results[c]["y"][: P["N"]]
    return out


if __name__ == "__main__":
    nc, io = build_program(False)
    print("built ok; instructions:", nc.n_instructions())
